# version 09
# speedup vs baseline: 2.5285x; 1.0093x over previous
import math
import numpy as np
import ml_dtypes
import concourse.bass as bass
import concourse.mybir as mybir
from concourse.bass_utils import run_bass_kernel_spmd
from contextlib import ExitStack

F32 = mybir.dt.float32
BF16 = mybir.dt.bfloat16
I32 = mybir.dt.int32
AF = mybir.ActivationFunctionType
ALU = mybir.AluOpType

D = 2048
TOW = 2048
TSEQ = 4096
NH = 16
DH = 128
KVH_B = 4
FF = 8192
NT = TOW // 128
NKT = D // 128
NKEYT = TSEQ // 128
SCALE = 1.0 / math.sqrt(DH)

N_SUB = 4
RACE_DETECT = True

KB = 1024
A_RES = 0
A_XL = 64 * KB
A_QB = 128 * KB
A_ST = 192 * KB
ARENA_SZ = 204 * KB

XL_MSK = 0
XL_KBUF = 8 * KB
XL_VT = 24 * KB
XL_PST = 40 * KB
XL_PMS = 44 * KB
XL_BC = 48 * KB
XL_PRAW = 50 * KB
XL_POUT = 54 * KB
XL_LNS = 32 * KB
XL_WRAW = 32 * KB
XL_RCS = 40 * KB
XL_WT = 48 * KB
XL_KVS = 56 * KB
XL_WD8 = 0
XL_WD = 2 * KB
XL_OSTG = 6 * KB
XL_DSB = 10 * KB
XL_TMP = 14 * KB

QB_LNS = 0
QB_GT = 0
QB_UT = 4 * KB
QB_GTC = 8 * KB
QB_UTC = 16 * KB
QB_HST = 24 * KB
QB_GACT = 28 * KB
QB_XSTG = 32 * KB

ST_QROW = 0
ST_IOTA = 2 * KB
ST_STATS = 4 * KB
ST_RT = 5 * KB
ST_CONST = 9 * KB
ST_ROW = 0


def _lhsT_tile(W):
    Kd, Md = W.shape
    W4 = W.reshape(Kd // 128, 128, Md // 128, 128)
    return np.ascontiguousarray(W4.transpose(2, 1, 0, 3).reshape(Md, Kd))


class EM:

    def __init__(self, marks, engines=None, sems=None):
        self.cnt = {}
        self.marks = marks
        self.engines = engines or {}
        self.sems = sems
        self.counting = sems is None

    def do(self, eng, fn, sem=None, n=1):
        e = self.engines.get(eng)
        if e is not None:
            inst = fn(e)
            if sem is not None:
                inst.then_inc(self.sems[sem], n)
        if sem is not None:
            self.cnt[sem] = self.cnt.get(sem, 0) + n

    def mark(self, key):
        if self.counting:
            self.marks[key] = dict(self.cnt)

    def wait(self, eng, key, sems):
        e = self.engines.get(eng)
        if e is None:
            return
        m = self.marks.get(key)
        assert m is not None, f"missing mark {key}"
        for s in sems:
            v = m.get(s, 0)
            if v > 0:
                e.wait_ge(self.sems[s], v)

    def selfwait(self, eng, sem):
        e = self.engines.get(eng)
        if e is not None:
            v = self.cnt.get(sem, 0)
            if v > 0:
                e.wait_ge(self.sems[sem], v)



def emit_program(em, A):
    subs = ["attn_a", "ffn", "attn_b", "ffn"][:N_SUB]
    ph_init(em, A)
    for i, kind in enumerate(subs):
        last = (i == len(subs) - 1)
        if kind in ("attn_a", "attn_b"):
            w = "a" if kind == "attn_a" else "b"
            ph_qkv(em, A, i, w)
            ph_ag(em, A, i, w)
            ph_precast_w(em, A, i, w)
            ph_scores(em, A, i, w)
            ph_outproj(em, A, i, w)
        else:
            ph_ln_ffn(em, A, i)
            ph_gu(em, A, i)
            ph_down(em, A, i, last)
    if subs[-1] != "ffn":
        ph_store_res(em, A)
    ph_final_wait(em, A)


def ph_init(em, A):
    for tt in range(NT):
        sl = tt % 2
        if tt >= 2:
            em.wait('sy', ("x_cv", tt - 2), ["act"])
        em.do('sy', lambda sy, tt=tt, sl=sl: sy.dma_start(
            out=A.xstg[:, sl * D:(sl + 1) * D],
            in_=A.x[tt * 128:(tt + 1) * 128, :]), sem="xl%d" % sl, n=16)
        em.mark(("x_ld", tt))
        em.wait('s', ("x_ld", tt), ["xl%d" % sl])
        em.do('s', lambda s, tt=tt, sl=sl: s.activation(
            out=A.RES[:, tt * D:(tt + 1) * D], in_=A.xstg[:, sl * D:(sl + 1) * D],
            func=AF.Identity), sem="act")
        em.mark(("x_cv", tt))
    for name in ["idt", "kcol", "gs", "us", "bqkv", "baseq", "onc_b", "onr_b",
                 "onr_f", "eps", "m128"]:
        sb = getattr(A, name)
        src = getattr(A, name + "_in")
        em.do('g', lambda g, sb=sb, src=src: g.dma_start(out=sb[:], in_=src[:, :]),
              sem="gld", n=16)
    em.selfwait('g', "gld")
    em.do('g', lambda g: g.sem_inc(A.sems["gpc"], 1), sem=None)
    em.cnt["gpc"] = em.cnt.get("gpc", 0) + 1
    em.mark(("init_done",))
    em.mark(("sub_done", -1))



def _ln_stats_group(em, A, sub, grp, stage):
    st = A.stats
    sl = lambda base, j: st[:, base + j: base + j + 1]
    SUM, SQ, MEAN, M2, RSTD, NMR = 0, 4, 8, 12, 16, 20
    tiles = [grp * 4 + j for j in range(4)]
    for j, tt in enumerate(tiles):
        res_t = A.RES[:, tt * D:(tt + 1) * D]
        stage_t = stage[:, j * D:(j + 1) * D]
        em.do('v', lambda v, res_t=res_t, stage_t=stage_t, j=j: v.tensor_scalar(
            out=stage_t, in0=res_t, scalar1=1.0, scalar2=0.0,
            op0=ALU.mult, op1=ALU.add, accum_out=sl(SUM, j)), sem="dve")
    for j, tt in enumerate(tiles):
        res_t = A.RES[:, tt * D:(tt + 1) * D]
        stage_t = stage[:, j * D:(j + 1) * D]
        em.do('v', lambda v, res_t=res_t, stage_t=stage_t, j=j:
              v.scalar_tensor_tensor(
                  out=stage_t, in0=res_t, scalar=1.0, in1=res_t,
                  op0=ALU.mult, op1=ALU.mult, accum_out=sl(SQ, j)), sem="dve")
    em.selfwait('v', "dve")
    for j in range(4):
        em.do('v', lambda v, j=j: v.tensor_scalar(
            out=sl(MEAN, j), in0=sl(SUM, j), scalar1=1.0 / D, scalar2=None,
            op0=ALU.mult), sem="dve")
    em.selfwait('v', "dve")
    for j in range(4):
        em.do('v', lambda v, j=j: v.tensor_tensor(
            out=sl(M2, j), in0=sl(MEAN, j), in1=sl(MEAN, j), op=ALU.mult),
            sem="dve")
    em.selfwait('v', "dve")
    for j in range(4):
        em.do('v', lambda v, j=j: v.scalar_tensor_tensor(
            out=sl(M2, j), in0=sl(SQ, j), scalar=1.0 / D, in1=sl(M2, j),
            op0=ALU.mult, op1=ALU.subtract), sem="dve")
    em.mark(("ln_var", sub, grp))
    em.wait('s', ("ln_var", sub, grp), ["dve"])
    for j in range(4):
        em.do('s', lambda s, j=j: s.activation(
            out=sl(RSTD, j), in_=sl(M2, j), func=AF.Sqrt, bias=A.eps[:]),
            sem="act")
    em.mark(("ln_std", sub, grp))
    em.wait('v', ("ln_std", sub, grp), ["act"])
    for j in range(4):
        em.do('v', lambda v, j=j: v.reciprocal(sl(RSTD, j), sl(RSTD, j)),
              sem="dve")
    em.selfwait('v', "dve")
    for j in range(4):
        em.do('v', lambda v, j=j: v.scalar_tensor_tensor(
            out=sl(NMR, j), in0=sl(MEAN, j), scalar=-1.0, in1=sl(RSTD, j),
            op0=ALU.mult, op1=ALU.mult), sem="dve")
    em.mark(("ln_ab", sub, grp))
    em.wait('s', ("ln_ab", sub, grp), ["dve"])
    for j, tt in enumerate(tiles):
        res_t = A.RES[:, tt * D:(tt + 1) * D]
        stage_t = stage[:, j * D:(j + 1) * D]
        em.do('s', lambda s, res_t=res_t, stage_t=stage_t, j=j: s.activation(
            out=stage_t, in_=res_t, func=AF.Identity,
            bias=sl(NMR, j), scale=sl(RSTD, j)), sem="act")
    em.mark(("ln_nat", sub, grp))


def _ln_group(em, A, sub, grp, stage, dst_col_of):
    _ln_stats_group(em, A, sub, grp, stage)
    for kt in range(NKT):
        pt = A.PT[kt % 2]
        em.wait('t', ("ln_nat", sub, grp), ["act"]) if kt == 0 else None
        if kt >= 2:
            em.wait('t', ("lnT_ev", sub, grp, kt - 2), ["act"])
        for j in range(4):
            src = stage[:, j * D + kt * 128: j * D + kt * 128 + 128]
            em.do('t', lambda t, pt=pt, j=j, src=src: t.transpose(
                pt[:, j * 128:(j + 1) * 128], src, A.idt[:]), sem="pe")
        em.mark(("lnT_mm", sub, grp, kt))
        em.wait('s', ("lnT_mm", sub, grp, kt), ["pe"])
        em.do('s', lambda s, pt=pt, kt=kt: s.activation(
            out=dst_col_of(kt), in_=pt[:, 0:512], func=AF.Identity), sem="act")
        em.mark(("lnT_ev", sub, grp, kt))


def ph_ln_ffn(em, A, sub):
    em.wait('v', ("sub_done", sub - 1), ["dve", "pe", "act", "gpc", "st0", "st1"])
    em.wait('s', ("sub_done", sub - 1), ["dve", "pe"])
    em.wait('t', ("sub_done", sub - 1), ["dve", "act"])
    for grp in range(4):
        if grp > 0:
            em.wait('v', ("lnT_grp", sub, grp - 1), ["pe"])
            em.wait('s', ("lnT_grp", sub, grp - 1), ["pe"])
        def dst_col_of(kt, grp=grp):
            return A.XL_bf[:, kt * D + grp * 512: kt * D + grp * 512 + 512]
        _ln_group(em, A, sub, grp, A.QB_lns, dst_col_of)
        em.mark(("lnT_grp", sub, grp))
    em.mark(("lnF_done", sub))



def ph_qkv(em, A, sub, which):
    if which == "a":
        mats = [("k", A.a_wk, NH, "k", 16), ("v", A.a_wv, NH, "v", 32),
                ("q", A.a_wq, NH, "qb", 0)]
        rope = False
        nkv = NH
        ag = A.ag_in_a
    else:
        mats = [("k", A.b_wk, KVH_B, "k", None), ("v", A.b_wv, KVH_B, "v", None),
                ("q", A.b_wq, NH, "qb", None)]
        rope = True
        nkv = KVH_B
        ag = A.ag_in_b
    prev = ("sub_done", sub - 1)
    em.wait('v', prev, ["dve", "pe", "act", "gpc", "st0", "st1"])
    em.wait('s', prev, ["dve", "pe"])
    em.wait('g', prev, ["dve", "pe", "act"])
    em.wait('t', prev, ["dve", "act"])

    mseq = 0
    last_kv = {}
    for half in range(2):
        if half == 1:
            em.wait('v', ("qkv_half", sub, 0), ["pe", "gst"])
            em.wait('s', ("qkv_half", sub, 0), ["pe", "gst"])
        for grp_l in range(2):
            grp = half * 2 + grp_l
            if grp > 0:
                em.wait('v', ("lnT_grp", sub, grp - 1), ["pe"])
                em.wait('s', ("lnT_grp", sub, grp - 1), ["pe"])
            def dst_col_of(kt, grp_l=grp_l):
                return A.XL_bf[:, kt * 1024 + grp_l * 512: kt * 1024 + grp_l * 512 + 512]
            _ln_group(em, A, sub, grp, A.XL_lns, dst_col_of)
            em.mark(("lnT_grp", sub, grp))
        em.mark(("lnA", sub, half))
        if rope:
            if half == 1:
                em.wait('sy', ("qkv_half", sub, 0), ["dve"])
            else:
                em.wait('sy', prev, ["dve", "pe"])
            em.wait('sy', ("lnA", sub, half), ["pe"])
            em.do('sy', lambda sy, half=half: sy.dma_start(
                out=A.rcs[:, 0:1024],
                in_=A.rcos[:, half * 1024:(half + 1) * 1024]), sem="rl", n=16)
            em.do('sy', lambda sy, half=half: sy.dma_start(
                out=A.rcs[:, 1024:2048],
                in_=A.rsin[:, half * 1024:(half + 1) * 1024]), sem="rl", n=16)
            em.mark(("rcs", sub, half))
        half_start = mseq
        flat = [(mat, w_ap, m, dstk, bias_base)
                for (mat, w_ap, nm, dstk, bias_base) in mats for m in range(nm)]
        nraw = 2 if which == "a" else 1

        def emit_load(ms):
            mat, w_ap, m, dstk, bias_base = flat[ms - half_start]
            rsl = ms % nraw
            if ms - nraw >= half_start:
                em.wait('g', ("qkv_wc", sub, ms - nraw), ["act"])
            if ms == half_start:
                em.wait('g', ("lnA", sub, half), ["pe"])
            em.do('g', lambda g, w_ap=w_ap, m=m, rsl=rsl: g.dma_start(
                out=A.wraw[:, rsl * D:(rsl + 1) * D],
                in_=w_ap[m * 128:(m + 1) * 128, :]),
                sem="wl%d" % (ms % 2), n=16)
            em.mark(("qkv_wld", sub, ms))

        def emit_conv(ms):
            rsl = ms % nraw
            wsl = ms % 2
            em.wait('s', ("qkv_wld", sub, ms), ["wl%d" % (ms % 2)])
            if ms - 2 >= half_start:
                em.wait('s', ("qkv_used", sub, ms - 2), ["pe"])
            em.do('s', lambda s, rsl=rsl, wsl=wsl: s.activation(
                out=A.wtile[:, wsl * D:(wsl + 1) * D],
                in_=A.wraw[:, rsl * D:(rsl + 1) * D],
                func=AF.Identity), sem="act")
            em.mark(("qkv_wc", sub, ms))

        emit_load(half_start)
        emit_conv(half_start)
        for (mat, w_ap, m, dstk, bias_base) in flat:
            if mseq + 1 < half_start + len(flat):
                emit_load(mseq + 1)
                emit_conv(mseq + 1)
            wsl = mseq % 2
            em.wait('t', ("qkv_wc", sub, mseq), ["act"])
            em.wait('t', ("lnA", sub, half), ["act"])
            banks = [A.PB[(2 * mseq + r) % 6] for r in range(2)]
            if mseq >= 3:
                em.wait('t', ("qkv_ev", sub, mseq - 3), ["act", "dve"])
            for kt in range(NKT):
                for tcl in range(2):
                    em.do('t', lambda t, kt=kt, tcl=tcl, banks=banks, wsl=wsl: t.matmul(
                        banks[tcl],
                        A.wtile[:, wsl * D + kt * 128: wsl * D + (kt + 1) * 128],
                        A.XL_bf[:, kt * 1024 + tcl * 512: kt * 1024 + (tcl + 1) * 512],
                        start=(kt == 0), stop=(kt == NKT - 1)), sem="pe")
            em.mark(("qkv_mm", sub, mseq))
            em.mark(("qkv_used", sub, mseq))
            for tcl in range(2):
                tc = half * 2 + tcl
                if dstk == "qb":
                    dst = A.QB_bf[:, m * TOW + tc * 512: m * TOW + (tc) * 512 + 512]
                else:
                    dst = A.kvstage[:, tcl * 512:(tcl + 1) * 512]
                pb = banks[tcl]
                em.wait('s', ("qkv_mm", sub, mseq), ["pe"])
                em.wait('v', ("qkv_mm", sub, mseq), ["pe"])
                if dstk != "qb" and half in last_kv:
                    em.wait('s', ("kv_st", sub, last_kv[half]), ["gst"])
                    em.wait('v', ("kv_st", sub, last_kv[half]), ["gst"])
                if which == "a":
                    bias_ap = A.bqkv[:, bias_base + m: bias_base + m + 1]
                    em.do('s', lambda s, dst=dst, pb=pb, bias_ap=bias_ap:
                          s.activation(out=dst, in_=pb, func=AF.Identity,
                                       bias=bias_ap), sem="act")
                elif dstk == "v":
                    em.do('s', lambda s, dst=dst, pb=pb:
                          s.activation(out=dst, in_=pb, func=AF.Identity),
                          sem="act")
                else:
                    em.wait('v', ("rcs", sub, half), ["rl"])
                    cos = A.rcs[0:64, tcl * 512:(tcl + 1) * 512]
                    sin = A.rcs[0:64, 1024 + tcl * 512: 1024 + (tcl + 1) * 512]
                    em.do('v', lambda v, pb=pb, cos=cos: v.tensor_tensor(
                        out=A.rt1[:], in0=pb[0:64, :], in1=cos, op=ALU.mult),
                        sem="dve")
                    em.do('v', lambda v, pb=pb, sin=sin: v.tensor_tensor(
                        out=A.rt2[:], in0=pb[64:128, :], in1=sin, op=ALU.mult),
                        sem="dve")
                    em.selfwait('v', "dve")
                    em.do('v', lambda v, dst=dst: v.tensor_tensor(
                        out=dst[0:64, :], in0=A.rt1[:], in1=A.rt2[:],
                        op=ALU.subtract), sem="dve")
                    em.do('v', lambda v, pb=pb, cos=cos: v.tensor_tensor(
                        out=A.rt1[:], in0=pb[64:128, :], in1=cos, op=ALU.mult),
                        sem="dve")
                    em.do('v', lambda v, pb=pb, sin=sin: v.tensor_tensor(
                        out=A.rt2[:], in0=pb[0:64, :], in1=sin, op=ALU.mult),
                        sem="dve")
                    em.selfwait('v', "dve")
                    em.do('v', lambda v, dst=dst: v.tensor_tensor(
                        out=dst[64:128, :], in0=A.rt1[:], in1=A.rt2[:],
                        op=ALU.add), sem="dve")
            em.mark(("qkv_ev", sub, mseq))
            if dstk in ("k", "v"):
                row0 = ((0 if dstk == "k" else 2 * nkv * 128) +
                        half * nkv * 128 + m * 128)
                em.wait('g', ("qkv_ev", sub, mseq), ["act", "dve"])
                em.do('g', lambda g, row0=row0, ag=ag: g.dma_start(
                    out=ag[row0:row0 + 128, 0:1024],
                    in_=A.kvstage[:, 0:1024]), sem="gst", n=16)
                em.mark(("kv_st", sub, mseq))
                last_kv[half] = mseq
                if m == nkv - 1:
                    em.mark((dstk + "_half_done", sub, half))
                    _emit_ag(em, A, sub, which, dstk, half)
            mseq += 1
        em.mark(("qkv_half", sub, half))
    em.mark(("qkv_done", sub))


def _emit_ag(em, A, sub, which, dstk, half):
    ag_in = A.ag_in_a if which == "a" else A.ag_in_b
    if dstk == "k":
        ag_out = A.ag_out_ka if which == "a" else A.ag_out_kb
    else:
        ag_out = A.ag_out_va if which == "a" else A.ag_out_vb
    rows = (NH if which == "a" else KVH_B) * 128
    in0 = (0 if dstk == "k" else 2 * rows) + half * rows
    out0 = half * 2 * rows
    em.wait('g', (dstk + "_half_done", sub, half), ["gst"])
    em.do('g', lambda g, ag_in=ag_in, ag_out=ag_out, in0=in0, out0=out0,
          rows=rows: g.collective_compute(
              "AllGather", ALU.bypass,
              replica_groups=[[0, 1], [2, 3], [4, 5], [6, 7]],
              ins=[ag_in[in0:in0 + rows, :]],
              outs=[ag_out[out0:out0 + 2 * rows, :]]), sem="cc")
    em.mark(("ag" + dstk + str(half), sub))


def ph_ag(em, A, sub, which):
    em.do('g', lambda g: g.iota(A.iota512[:], [[1, 512]], base=0,
                                channel_multiplier=0,
                                allow_small_or_imprecise_dtypes=True), sem="gpc")
    em.mark(("iot", sub))


def ph_precast_w(em, A, sub, which):
    src = A.a_wo if which == "a" else A.b_wp
    dst = A.wo_bf if which == "a" else A.wp_bf
    em.wait('g', ("qkv_done", sub), ["pe"])
    for r in range(32):
        row = r // 2
        col = (r % 2) * 1024
        em.do('g', lambda g, row=row, col=col, src=src: g.dma_start(
            out=A.praw[:], in_=src[row * 128:(row + 1) * 128, col:col + 1024]),
            sem="gld", n=16)
        em.selfwait('g', "gld")
        em.do('g', lambda g: g.tensor_copy(out=A.pout[:], in_=A.praw[:]),
              sem="gpc")
        em.selfwait('g', "gpc")
        em.do('g', lambda g, row=row, col=col, dst=dst: g.dma_start(
            out=dst[row * 128:(row + 1) * 128, col:col + 1024], in_=A.pout[:]),
            sem="gst", n=16)
        em.selfwait('g', "gst")
    em.do('g', lambda g: g.sem_inc(A.sems["gpc"], 1), sem=None)
    em.cnt["gpc"] = em.cnt.get("gpc", 0) + 1
    em.mark(("wop", sub))


def ph_scores(em, A, sub, which):
    nq = NH
    grp_ratio = 1 if which == "a" else 4
    nkv = NH if which == "a" else KVH_B
    ag_out_k = A.ag_out_ka if which == "a" else A.ag_out_kb
    ag_out_v = A.ag_out_va if which == "a" else A.ag_out_vb
    blk = nkv * 128
    em.mark(("scoreshead", sub))
    fi = 0
    kb_cur = 0
    for qt in range(4):
        n4 = 4 * (qt + 1)
        qlist = list(range(qt + 1)) + list(range(4, 4 + qt + 1))
        plist = ([(t, t + 1) for t in range(0, n4, 2)] +
                 [(16 + t, 16 + t + 1) for t in range(0, n4, 2)])
        NKPq = len(plist)
        mtiles = ([4 * qt + j for j in range(4)] +
                  [16 + 4 * qt + j for j in range(4)])

        def mslot_of(t0, qt=qt):
            if t0 < 16:
                return (t0 - 4 * qt) if t0 >= 4 * qt else None
            return (4 + t0 - 16 - 4 * qt) if t0 - 16 >= 4 * qt else None
        if qt == 0:
            em.wait('v', ("iot", sub), ["gpc"])
            em.wait('v', ("qkv_done", sub), ["pe"])
        em.do('v', lambda v, qt=qt: v.tensor_scalar(
            out=A.qrow[:], in0=A.iota512[:], scalar1=A.baseq[:, qt:qt + 1],
            scalar2=0.0, op0=ALU.add, op1=ALU.add), sem="dve")
        em.selfwait('v', "dve")
        for mi, ktg in enumerate(mtiles):
            em.do('v', lambda v, mi=mi, ktg=ktg: v.tensor_scalar(
                out=A.MSK[:, mi * 512:(mi + 1) * 512], in0=A.qrow[:],
                scalar1=A.kcol[:, ktg:ktg + 1], scalar2=None,
                op0=ALU.is_ge), sem="dve")
        em.mark(("masks", sub, qt))
        for h in range(nq):
            kvh = h // grp_ratio
            fresh = (h % grp_ratio) == 0
            it = qt * nq + h
            prev_it = it - 1
            if fresh:
                sl = fi % 2
                kb_cur = sl * 4096
                kb = kb_cur
                if fi >= 2:
                    pfi = fi - 2
                    if which == "a":
                        used_key = ("kv_used", sub, pfi // nq, pfi % nq)
                    else:
                        used_key = ("kv_used", sub, pfi // 4, (pfi % 4) * 4 + 3)
                    em.wait('sy', used_key, ["pe"])
                elif fi == 0:
                    em.wait('sy', ("qkv_done", sub), ["act", "dve", "pe"])
                    em.wait('sy', ("agk0", sub), ["cc"])
                if qt == 2 and h == 0:
                    em.wait('sy', ("agk1", sub), ["cc"])

                def ld_pieces(buf, ag_out):
                    for b in range(2):
                        w0 = min(n4 * 128, 1024)
                        em.do('sy', lambda sy, b=b, kvh=kvh, ag_out=ag_out,
                              blk=blk, w0=w0, buf=buf, kb=kb: sy.dma_start(
                                  out=buf[:, kb + b * 2048:
                                          kb + b * 2048 + w0],
                                  in_=ag_out[b * blk + kvh * 128:
                                             b * blk + kvh * 128 + 128, 0:w0]),
                              sem="kvl", n=16)
                        if n4 * 128 > 1024:
                            w1 = n4 * 128 - 1024
                            em.do('sy', lambda sy, b=b, kvh=kvh, ag_out=ag_out,
                                  blk=blk, w1=w1, buf=buf, kb=kb: sy.dma_start(
                                      out=buf[:, kb + b * 2048 + 1024:
                                              kb + b * 2048 + 1024 + w1],
                                      in_=ag_out[2 * blk + b * blk + kvh * 128:
                                                 2 * blk + b * blk + kvh * 128
                                                 + 128, 0:w1]),
                                  sem="kvl", n=16)
                ld_pieces(A.kbuf, ag_out_k)
                em.mark(("k_ld", sub, qt, h))
                if fi == 0:
                    em.wait('sy', ("agv0", sub), ["cc"])
                if qt == 2 and h == 0:
                    em.wait('sy', ("agv1", sub), ["cc"])
                ld_pieces(A.vtbuf, ag_out_v)
                em.mark(("kv_ld", sub, qt, h))
                em.wait('t', ("kv_ld", sub, qt, h), ["kvl"])
                if it == 0:
                    em.wait('t', ("qkv_done", sub), ["act", "dve"])
                for qi, quad in enumerate(qlist):
                    pt = A.PT[qi % 2]
                    if qi >= 2:
                        em.wait('t', ("vt_ev", sub, qt, h, qlist[qi - 2]),
                                ["act", "dve"])
                    for j in range(4):
                        vk = quad * 4 + j
                        em.do('t', lambda t, pt=pt, j=j, vk=vk, kb=kb: t.transpose(
                            pt[:, j * 128:(j + 1) * 128],
                            A.vtbuf[:, kb + vk * 128:kb + (vk + 1) * 128],
                            A.idt[:]), sem="pe")
                    em.mark(("vt_mm", sub, qt, h, quad))
                    eng = 's' if qi % 2 == 0 else 'v'
                    em.wait(eng, ("vt_mm", sub, qt, h, quad), ["pe"])
                    if eng == 's':
                        em.do('s', lambda s, pt=pt, quad=quad, kb=kb: s.activation(
                            out=A.vtbuf[:, kb + quad * 512:kb + (quad + 1) * 512],
                            in_=pt[:, 0:512], func=AF.Identity), sem="act")
                    else:
                        em.do('v', lambda v, pt=pt, quad=quad, kb=kb: v.tensor_copy(
                            out=A.vtbuf[:, kb + quad * 512:kb + (quad + 1) * 512],
                            in_=pt[:, 0:512]), sem="dve")
                    em.mark(("vt_ev", sub, qt, h, quad))
                em.mark(("vt", sub, qt, h))
                fi += 1
            kb = kb_cur
            vt_key = ("vt", sub, qt, (h // grp_ratio) * grp_ratio)
            qcols = slice(h * TOW + qt * 512, h * TOW + qt * 512 + 512)

            def emit_smm(kp):
                pr = plist[kp]
                ms = mslot_of(pr[0])
                S = A.SA if kp % 2 == 0 else A.SB
                if kp >= 2:
                    em.wait('t', ("exp", sub, it, kp - 2), ["act"])
                for j, kt in enumerate(pr):
                    em.do('t', lambda t, S=S, j=j, kt=kt, qcols=qcols, kb=kb:
                          t.matmul(
                              S[:, j * 512:(j + 1) * 512],
                              A.kbuf[:, kb + kt * 128:kb + (kt + 1) * 128],
                              A.QB_bf[:, qcols], start=True, stop=True), sem="pe")
                em.mark(("smm", sub, it, kp))
                em.wait('s', ("smm", sub, it, kp), ["pe"])
                if kp >= 2:
                    if mslot_of(plist[kp - 2][0]) is not None:
                        em.wait('s', ("msk", sub, it, kp - 2), ["dve"])
                    else:
                        em.wait('s', ("pv", sub, it, kp - 2), ["pe"])
                pout = A.pstg[:, (kp % 2) * 1024:(kp % 2 + 1) * 1024]
                S2 = A.SA if kp % 2 == 0 else A.SB
                em.do('s', lambda s, S2=S2, pout=pout: s.activation(
                    out=pout, in_=S2[:], func=AF.Exp, scale=SCALE), sem="act")
                em.mark(("exp", sub, it, kp))
                em.wait('v', ("exp", sub, it, kp), ["act"])
                if kp >= 2:
                    em.wait('v', ("pv", sub, it, kp - 2), ["pe"])
                if ms is not None:
                    em.wait('v', ("masks", sub, qt), ["dve"]) if kp == 0 and h == 0 else None
                    pm = A.pmstg[:, (kp % 2) * 1024:(kp % 2 + 1) * 1024]
                    em.do('v', lambda v, ms=ms, pout=pout, pm=pm: v.tensor_tensor(
                        out=pm, in0=pout,
                        in1=A.MSK[:, ms * 512:(ms + 2) * 512], op=ALU.mult),
                        sem="dve")
                    em.mark(("msk", sub, it, kp))
                    dsrc = A.pmstg
                else:
                    dsrc = A.pstg
                em.do('v', lambda v, dsrc=dsrc, kp=kp: v.tensor_tensor(
                    out=A.tsum[:, (kp % 2) * 512:(kp % 2 + 1) * 512],
                    in0=dsrc[:, (kp % 2) * 1024:(kp % 2) * 1024 + 512],
                    in1=dsrc[:, (kp % 2) * 1024 + 512:(kp % 2 + 1) * 1024],
                    op=ALU.add), sem="dve")
                if kp == 0:
                    em.wait('v', ("den", sub, it - 1), ["pe"]) if it > 0 else None
                    em.do('v', lambda v: v.tensor_copy(
                        out=A.dacc[:], in_=A.tsum[:, 0:512]), sem="dve")
                else:
                    em.do('v', lambda v, kp=kp: v.tensor_tensor(
                        out=A.dacc[:], in0=A.dacc[:],
                        in1=A.tsum[:, (kp % 2) * 512:(kp % 2 + 1) * 512],
                        op=ALU.add), sem="dve")
                em.mark(("dacc", sub, it, kp))

            def emit_pv(kp):
                pr = plist[kp]
                ms = mslot_of(pr[0])
                src = A.pmstg if ms is not None else A.pstg
                if ms is not None:
                    em.wait('t', ("msk", sub, it, kp), ["dve"])
                else:
                    em.wait('t', ("exp", sub, it, kp), ["act"])
                if kp == 0 and it > 0:
                    em.wait('t', ("poc", sub, it - 1), ["dve"])
                for j, kt in enumerate(pr):
                    pmj = src[:, (kp % 2) * 1024 + j * 512:
                              (kp % 2) * 1024 + (j + 1) * 512]
                    first = (kp == 0 and j == 0)
                    last = (kp == NKPq - 1 and j == 1)
                    em.do('t', lambda t, kt=kt, pmj=pmj, first=first, last=last,
                          kb=kb: t.matmul(
                              A.PO[:], A.vtbuf[:, kb + kt * 128:kb + (kt + 1) * 128],
                              pmj, start=first, stop=last), sem="pe")
                em.mark(("pv", sub, it, kp))

            for kp in range(NKPq + 1):
                if kp < NKPq:
                    if kp == 0:
                        em.wait('t', vt_key, ["act", "dve"])
                    emit_smm(kp)
                if kp == 1 and it > 0:
                    _emit_tail(em, A, sub, it - 1)
                if kp >= 1:
                    emit_pv(kp - 1)
            em.mark(("kv_used", sub, qt, h))
            em.wait('t', ("dacc", sub, it, NKPq - 1), ["dve"])
            if it > 0:
                em.wait('t', ("bc_ev", sub, it - 1), ["act"])
            em.do('t', lambda t: t.matmul(
                A.PD[0:1, :], A.onc_b[:], A.dacc[:],
                start=True, stop=True), sem="pe")
            em.mark(("den", sub, it))
            em.wait('v', ("pv", sub, it, NKPq - 1), ["pe"])
            em.do('v', lambda v: v.tensor_copy(out=A.postg[:], in_=A.PO[:]),
                  sem="dve")
            em.mark(("poc", sub, it))
            em.wait('s', ("den", sub, it), ["pe"])
            em.do('s', lambda s: s.activation(
                out=A.recip[:], in_=A.PD[0:1, :], func=AF.Ln), sem="act")
            em.do('s', lambda s: s.activation(
                out=A.recipb[:], in_=A.recip[:], func=AF.Exp, scale=-1.0),
                sem="act")
            em.mark(("recip", sub, it))
    _emit_tail(em, A, sub, 4 * nq - 1)
    em.mark(("scores_done", sub))


def _emit_tail(em, A, sub, jt):
    nq = NH
    qcols = slice((jt % nq) * TOW + (jt // nq) * 512,
                  (jt % nq) * TOW + (jt // nq) * 512 + 512)
    em.wait('t', ("recip", sub, jt), ["act"])
    em.do('t', lambda t: t.matmul(
        A.PD[:], A.onr_b[:], A.recipb[:], start=True, stop=True), sem="pe")
    em.mark(("bc_mm", sub, jt))
    em.wait('s', ("bc_mm", sub, jt), ["pe"])
    em.do('s', lambda s: s.activation(
        out=A.bcast[:], in_=A.PD[:], func=AF.Identity), sem="act")
    em.mark(("bc_ev", sub, jt))
    em.wait('v', ("bc_ev", sub, jt), ["act"])
    em.do('v', lambda v, qcols=qcols: v.tensor_tensor(
        out=A.QB_bf[:, qcols], in0=A.postg[:], in1=A.bcast[:],
        op=ALU.mult), sem="dve")
    em.mark(("epi", sub, jt))


def ph_outproj(em, A, sub, which):
    w_ap = A.wo_bf if which == "a" else A.wp_bf
    em.wait('sy', ("scores_done", sub), ["pe", "act", "dve"])
    em.wait('sy', ("wop", sub), ["gpc"])
    if which == "a":
        em.wait('g', ("scores_done", sub), ["dve"])
        em.do('g', lambda g: g.dma_start(out=A.row_b[:], in_=A.a_bo[:, :]),
              sem="gld", n=16)
        em.mark(("bo_ld", sub))
    em.do('sy', lambda sy, w_ap=w_ap: sy.dma_start(
        out=A.XL_bf.ap().rearrange("p (k c) -> p k c", c=D),
        in_=w_ap.rearrange("(k p) c -> p k c", p=128)), sem="wol", n=16)
    em.mark(("wo_ld", sub))
    for tt in range(NT):
        banks = [A.PB[(4 * tt + r) % 6] for r in range(4)]
        if tt == 0:
            em.wait('t', ("wo_ld", sub), ["wol"])
            em.wait('t', ("scores_done", sub), ["dve", "act"])
        else:
            em.wait('t', ("op_ev", sub, tt - 1), ["dve"])
        for kt in range(NKT):
            for dch in range(4):
                em.do('t', lambda t, kt=kt, dch=dch, tt=tt, banks=banks: t.matmul(
                    banks[dch],
                    A.QB_bf[:, kt * TOW + tt * 128: kt * TOW + tt * 128 + 128],
                    A.XL_bf[:, kt * D + dch * 512: kt * D + (dch + 1) * 512],
                    start=(kt == 0),
                    stop=(kt == NKT - 1 and which != "a")), sem="pe")
        if which == "a":
            if tt == 0:
                em.wait('t', ("bo_ld", sub), ["gld"])
            for dch in range(4):
                em.do('t', lambda t, dch=dch, banks=banks: t.matmul(
                    banks[dch], A.onr_b[:],
                    A.row_b[0:1, dch * 512:(dch + 1) * 512],
                    start=False, stop=True), sem="pe")
        em.mark(("op_mm", sub, tt))
        em.wait('v', ("op_mm", sub, tt), ["pe"])
        for dch in range(4):
            res_sl = A.RES[:, tt * D + dch * 512: tt * D + (dch + 1) * 512]
            em.do('v', lambda v, res_sl=res_sl, dch=dch, banks=banks: v.tensor_tensor(
                out=res_sl, in0=banks[dch], in1=res_sl, op=ALU.add), sem="dve")
        em.mark(("op_ev", sub, tt))
    em.mark(("sub_done", sub))



def ph_gu(em, A, sub):
    em.wait('g', ("lnF_done", sub), ["act", "pe"])
    NI = FF // 128

    def emit_load(gi):
        it = gi % NI
        gsl = gi % 2
        if gi >= 2:
            em.wait('g', ("gu_used", sub, gi - 2), ["pe"])
        em.do('g', lambda g, it=it, gsl=gsl: g.dma_start(
            out=A.gtile[:, gsl * D:(gsl + 1) * D],
            in_=A.g_w[it * 128:(it + 1) * 128, :]), sem="gl%d" % gsl, n=16)
        em.do('g', lambda g, it=it, gsl=gsl: g.dma_start(
            out=A.utile[:, gsl * D:(gsl + 1) * D],
            in_=A.u_w[it * 128:(it + 1) * 128, :]), sem="gl%d" % gsl, n=16)
        em.mark(("gu_wld", sub, gi))

    def emit_conv(gi):
        gsl = gi % 2
        em.wait('s', ("gu_wld", sub, gi), ["gl%d" % gsl])
        if gi >= 2:
            em.wait('s', ("gu_used", sub, gi - 2), ["pe"])
        for (src, dst) in ((A.gtile, A.gtile_c), (A.utile, A.utile_c)):
            sl_ = slice(gsl * D, (gsl + 1) * D)
            em.do('s', lambda s, src=src, dst=dst, sl_=sl_: s.activation(
                out=dst[:, sl_], in_=src[:, sl_], func=AF.Identity,
                bias=A.m128[:]), sem="act")
        em.mark(("gu_wc", sub, gi))

    emit_load(0)
    emit_conv(0)
    for p2 in range(2):
        for it in range(NI):
            gi = p2 * NI + it
            gsl = gi % 2
            if gi + 1 < 2 * NI:
                emit_load(gi + 1)
                emit_conv(gi + 1)
            banks = [A.PB[(4 * gi + r) % 6] for r in range(4)]
            em.wait('t', ("gu_wc", sub, gi), ["act"])
            if gi == 0:
                em.wait('t', ("lnF_done", sub), ["act"])
                em.wait('t', ("sub_done", sub - 1), ["dve"])
            else:
                em.wait('t', ("gu_silu", sub, gi - 1, 1), ["act"])
                if gi >= 2:
                    em.wait('t', ("gu_ev", sub, gi - 2), ["dve"])
            for kt in range(NKT):
                for tcl in range(2):
                    tc = p2 * 2 + tcl
                    xsl = A.XL_bf[:, kt * D + tc * 512: kt * D + (tc + 1) * 512]
                    em.do('t', lambda t, kt=kt, xsl=xsl, banks=banks, tcl=tcl, gsl=gsl:
                          t.matmul(banks[tcl],
                                   A.gtile_c[:, gsl * D + kt * 128: gsl * D + (kt + 1) * 128],
                                   xsl,
                                   start=(kt == 0), stop=(kt == NKT - 1)), sem="pe")
                    em.do('t', lambda t, kt=kt, xsl=xsl, banks=banks, tcl=tcl, gsl=gsl:
                          t.matmul(banks[2 + tcl],
                                   A.utile_c[:, gsl * D + kt * 128: gsl * D + (kt + 1) * 128],
                                   xsl,
                                   start=(kt == 0), stop=(kt == NKT - 1)), sem="pe")
            em.mark(("gu_mm", sub, gi))
            em.mark(("gu_used", sub, gi))
            for tcl in range(2):
                em.wait('s', ("gu_mm", sub, gi), ["pe"]) if tcl == 0 else None
                if gi >= 1 and tcl == 0:
                    em.wait('s', ("h_st", sub, gi - 1), ["gst"])
                em.do('s', lambda s, tcl=tcl, it=it, banks=banks: s.activation(
                    out=A.gact[:, tcl * 512:(tcl + 1) * 512], in_=banks[tcl],
                    func=AF.Silu, scale=A.gs[:, it:it + 1]), sem="act")
                em.mark(("gu_silu", sub, gi, tcl))
                em.wait('v', ("gu_silu", sub, gi, tcl), ["act"])
                if gi >= 1 and tcl == 0:
                    em.wait('v', ("h_st", sub, gi - 1), ["gst"])
                em.do('v', lambda v, tcl=tcl, it=it, banks=banks:
                      v.scalar_tensor_tensor(
                          out=A.hstage[:, tcl * 512:(tcl + 1) * 512],
                          in0=banks[2 + tcl], scalar=A.us[:, it:it + 1],
                          in1=A.gact[:, tcl * 512:(tcl + 1) * 512],
                          op0=ALU.mult, op1=ALU.mult), sem="dve")
            em.mark(("gu_ev", sub, gi))
            em.wait('g', ("gu_ev", sub, gi), ["act", "dve"])
            em.do('g', lambda g, it=it, p2=p2: g.dma_start(
                out=A.h_sp[it * 128:(it + 1) * 128, p2 * 1024:(p2 + 1) * 1024],
                in_=A.hstage[:]), sem="gst", n=16)
            em.mark(("h_st", sub, gi))
        em.selfwait('g', "gst")
        em.do('g', lambda g: g.sem_inc(A.sems["gpc"], 1), sem=None)
        em.cnt["gpc"] = em.cnt.get("gpc", 0) + 1
        em.mark(("h_ck", sub, p2))
    em.mark(("gu_done", sub))


def ph_down(em, A, sub, last):
    NIK = FF // 128
    em.wait('g', ("gu_done", sub), ["pe"])
    em.do('g', lambda g: g.dma_start(out=A.dsb[:], in_=A.dsb_in[:, :]),
          sem="gld", n=16)
    em.selfwait('g', "gld")
    em.do('g', lambda g: g.sem_inc(A.sems["gpc"], 1), sem=None)
    em.cnt["gpc"] = em.cnt.get("gpc", 0) + 1
    em.mark(("dsb_ld", sub))
    for tc in range(4):
        p2 = tc // 2
        need = ("h_ck", sub, 0) if p2 == 0 else ("h_ck", sub, 1)
        em.wait('g', need, ["gpc"])
        if tc == 0:
            em.wait('g', ("gu_done", sub), ["pe", "dve", "act"])
        nik8 = FF // 128 // 8
        for part in range(8):
            if tc > 0:
                em.wait('g', ("wd_used", sub, tc * 4 - 1, part * nik8 + nik8 - 1),
                        ["pe"])
            em.do('g', lambda g, tc=tc, part=part, nik8=nik8: g.dma_start(
                out=A.QB_bf[:, part * nik8 * 512:(part + 1) * nik8 * 512]
                    .rearrange("p (i c) -> p i c", c=512),
                in_=A.h_sp.rearrange("(i p) t -> p i t", p=128)
                    [:, part * nik8:(part + 1) * nik8, tc * 512:(tc + 1) * 512]),
                sem="hl", n=16)
            em.mark(("hslp", sub, tc, part))
        em.mark(("hsl", sub, tc))
        for dq in range(4):
            di = tc * 4 + dq
            banks = [A.PB[(4 * di + r) % 6] for r in range(4)]
            for ik in range(NIK):
                wslot = ik % 4
                if di == 0 and ik < 4:
                    em.wait('sy', ("gu_done", sub), ["pe"])
                if ik >= 4:
                    em.wait('sy', ("wd_used", sub, di, ik - 4), ["pe"])
                elif di > 0:
                    em.wait('sy', ("wd_used", sub, di - 1, NIK - 4 + ik), ["pe"])
                em.do('sy', lambda sy, ik=ik, dq=dq, wslot=wslot: sy.dma_start(
                    out=A.wdt8[:, wslot * 512:(wslot + 1) * 512],
                    in_=A.d_w[ik * 128:(ik + 1) * 128, dq * 512:(dq + 1) * 512]),
                    sem="dl%d" % wslot, n=16)
                em.mark(("wd_ld", sub, di, ik))
                em.wait('s', ("wd_ld", sub, di, ik), ["dl%d" % wslot])
                if ik >= 4:
                    em.wait('s', ("wd_used", sub, di, ik - 4), ["pe"])
                elif di > 0:
                    em.wait('s', ("wd_used", sub, di - 1, NIK - 4 + ik), ["pe"])
                em.do('s', lambda s, wslot=wslot: s.activation(
                    out=A.wdt[:, wslot * 512:(wslot + 1) * 512],
                    in_=A.wdt8[:, wslot * 512:(wslot + 1) * 512],
                    func=AF.Identity, bias=A.m128[:]), sem="act")
                em.mark(("wd_wc", sub, di, ik))
                em.wait('t', ("wd_wc", sub, di, ik), ["act"])
                if di == tc * 4 and (ik % (FF // 128 // 8)) == 0:
                    em.wait('t', ("hslp", sub, tc, ik // (FF // 128 // 8)), ["hl"])
                if ik == 0:
                    if di > 0:
                        em.wait('t', ("dn_ev", sub, di - 1), ["dve"])
                    else:
                        em.wait('t', ("gu_done", sub), ["act", "dve"])
                    if di > tc * 4:
                        em.wait('t', ("hsl", sub, tc), ["hl"])
                for mt in range(4):
                    em.do('t', lambda t, mt=mt, ik=ik, wslot=wslot, banks=banks:
                          t.matmul(banks[mt],
                                   A.QB_bf[:, ik * 512 + mt * 128:
                                           ik * 512 + mt * 128 + 128],
                                   A.wdt[:, wslot * 512:(wslot + 1) * 512],
                                   start=(ik == 0), stop=(ik == NIK - 1)), sem="pe")
                em.mark(("wd_used", sub, di, ik))
            em.mark(("dn_mm", sub, di))
            em.wait('v', ("dn_mm", sub, di), ["pe"])
            em.wait('v', ("dsb_ld", sub), ["gpc"]) if di == 0 else None
            for mt in range(4):
                tt = tc * 4 + mt
                res_sl = A.RES[:, tt * D + dq * 512: tt * D + (dq + 1) * 512]
                dsb_sl = A.dsb[:, dq * 512:(dq + 1) * 512]
                em.do('v', lambda v, mt=mt, banks=banks, dsb_sl=dsb_sl:
                      v.tensor_tensor(out=A.dtmp[:], in0=banks[mt],
                                      in1=dsb_sl, op=ALU.mult), sem="dve")
                em.selfwait('v', "dve")
                if not last:
                    em.do('v', lambda v, res_sl=res_sl: v.tensor_tensor(
                        out=res_sl, in0=A.dtmp[:], in1=res_sl, op=ALU.add),
                        sem="dve")
                else:
                    oslot = mt % 2
                    pi = di * 4 + mt
                    if pi >= 2:
                        em.wait('v', ("y_st", sub, pi - 2), ["st%d" % (pi % 2)])
                    em.do('v', lambda v, res_sl=res_sl, oslot=oslot:
                          v.tensor_tensor(
                              out=A.ostg[:, oslot * 512:(oslot + 1) * 512],
                              in0=A.dtmp[:], in1=res_sl, op=ALU.add), sem="dve")
                    em.mark(("y_ev", sub, pi))
                    em.wait('sy', ("y_ev", sub, pi), ["dve"])
                    em.do('sy', lambda sy, tt=tt, dq=dq, oslot=oslot: sy.dma_start(
                        out=A.y[tt * 128:(tt + 1) * 128, dq * 512:(dq + 1) * 512],
                        in_=A.ostg[:, oslot * 512:(oslot + 1) * 512]),
                        sem="st%d" % (pi % 2), n=16)
                    em.mark(("y_st", sub, pi))
            em.mark(("dn_ev", sub, di))
    em.mark(("sub_done", sub))


def ph_store_res(em, A):
    sub = N_SUB - 1
    em.wait('s', ("sub_done", sub), ["dve", "pe"])
    for pi in range(NT * 4):
        tt, dch = pi // 4, pi % 4
        slot = pi % 2
        if pi >= 2:
            em.wait('s', ("yr_st", pi - 2), ["st%d" % (pi % 2)])
        em.do('s', lambda s, tt=tt, dch=dch, slot=slot: s.activation(
            out=A.ostg[:, slot * 512:(slot + 1) * 512],
            in_=A.RES[:, tt * D + dch * 512: tt * D + (dch + 1) * 512],
            func=AF.Identity), sem="act")
        em.mark(("yr_ev", pi))
        em.wait('sy', ("yr_ev", pi), ["act"])
        em.do('sy', lambda sy, tt=tt, dch=dch, slot=slot: sy.dma_start(
            out=A.y[tt * 128:(tt + 1) * 128, dch * 512:(dch + 1) * 512],
            in_=A.ostg[:, slot * 512:(slot + 1) * 512]), sem="st%d" % (pi % 2), n=16)
        em.mark(("yr_st", pi))


def ph_final_wait(em, A):
    em.mark(("all_stores",))
    em.wait('sy', ("all_stores",), ["st0", "st1"])



class NS:
    pass


def build_nc():
    nc = bass.Bass("TRN2", target_bir_lowering=False,
                   detect_race_conditions=RACE_DETECT)
    A = NS()
    A.nc = nc

    def din(name, shape, dt):
        ap = nc.dram_tensor(name, shape, dt, kind="ExternalInput").ap()
        setattr(A, name, ap)
        return ap
    din("x", [TOW, D], F32)
    din("a_wq", [D, D], F32)
    din("a_wk", [D, D], F32)
    din("a_wv", [D, D], F32)
    din("a_wo", [D, D], F32)
    din("a_bo", [1, D], F32)
    din("b_wq", [D, D], F32)
    din("b_wk", [512, D], F32)
    din("b_wv", [512, D], F32)
    din("b_wp", [D, D], F32)
    din("g_w", [FF, D], mybir.dt.uint8)
    din("u_w", [FF, D], mybir.dt.uint8)
    din("d_w", [FF, D], mybir.dt.uint8)
    din("dsb_in", [128, D], F32)
    din("rcos", [64, TOW], F32)
    din("rsin", [64, TOW], F32)
    din("idt_in", [128, 128], BF16)
    din("kcol_in", [128, 32], F32)
    din("gs_in", [128, 64], F32)
    din("us_in", [128, 64], F32)
    din("bqkv_in", [128, 48], F32)
    din("baseq_in", [128, 4], F32)
    din("onc_b_in", [128, 1], BF16)
    din("onr_b_in", [1, 128], BF16)
    din("onr_f_in", [1, 128], F32)
    din("eps_in", [128, 1], F32)
    din("m128_in", [128, 1], F32)
    A.y = nc.dram_tensor("y", [TOW, D], F32, kind="ExternalOutput").ap()

    A.ag_in_a = nc.dram_tensor("ag_in_a", [8 * D, 1024], BF16, kind="Internal").ap()
    A.ag_out_ka = nc.dram_tensor("ag_out_ka", [4 * D, 1024], BF16, kind="Internal").ap()
    A.ag_out_va = nc.dram_tensor("ag_out_va", [4 * D, 1024], BF16, kind="Internal").ap()
    A.ag_in_b = nc.dram_tensor("ag_in_b", [4096, 1024], BF16, kind="Internal").ap()
    A.ag_out_kb = nc.dram_tensor("ag_out_kb", [2048, 1024], BF16, kind="Internal").ap()
    A.ag_out_vb = nc.dram_tensor("ag_out_vb", [2048, 1024], BF16, kind="Internal").ap()
    A.wo_bf = nc.dram_tensor("wo_bf", [D, D], BF16, kind="Internal").ap()
    A.wp_bf = nc.dram_tensor("wp_bf", [D, D], BF16, kind="Internal").ap()
    A.h_sp = nc.dram_tensor("h_sp", [FF, TOW], BF16, kind="Internal").ap()

    ctx = ExitStack()
    arena = ctx.enter_context(nc.sbuf_tensor("ARENA", [128, ARENA_SZ],
                                             mybir.dt.uint8))
    base = nc.lookup_mloc(arena).addr

    def at(name, shape, dtype, off):
        h = nc.virtual_sbuf_tensor(name, shape, dtype)
        nc._tensor(h.name, shape, dtype, type="SB")
        mloc = nc.lookup_mloc(h)
        mloc.allocated = True
        mloc.addr = base + off
        return h

    A.RES = at("RES", [128, 16 * D], BF16, A_RES)
    A.XL_bf = at("XL_bf", [128, 16 * D], BF16, A_XL)
    A.QB_bf = at("QB_bf", [128, 16 * D], BF16, A_QB)
    A.MSK = at("MSK", [128, 8 * 512], BF16, A_XL + XL_MSK)
    A.kbuf = at("kbuf", [128, 8192], BF16, A_XL + XL_KBUF)
    A.vtbuf = at("vtbuf", [128, 8192], BF16, A_XL + XL_VT)
    A.praw = at("praw", [128, 1024], F32, A_XL + XL_PRAW)
    A.pout = at("pout", [128, 1024], BF16, A_XL + XL_POUT)
    A.bcast = at("bcast", [128, 512], F32, A_XL + XL_BC)
    A.pstg = at("pstg", [128, 2048], BF16, A_XL + XL_PST)
    A.pmstg = at("pmstg", [128, 2048], BF16, A_XL + XL_PMS)
    A.tsum = at("tsum", [128, 1024], BF16, A_XL + 58 * KB)
    A.dacc = at("dacc", [128, 512], BF16, A_XL + 60 * KB)
    A.postg = at("postg", [128, 512], F32, A_XL + 61 * KB)
    A.XL_lns = at("XL_lns", [128, 4 * D], BF16, A_XL + XL_LNS)
    A.wtile = at("wtile", [128, 2 * D], BF16, A_XL + XL_WT)
    A.wraw = at("wraw", [128, 2 * D], F32, A_XL + XL_WRAW)
    A.kvstage = at("kvstage", [128, 1024], BF16, A_XL + XL_KVS)
    A.rcs = at("rcs", [64, 2048], F32, A_XL + XL_RCS)
    A.wdt8 = at("wdt8", [128, 2048], mybir.dt.uint8, A_XL + XL_WD8)
    A.wdt = at("wdt", [128, 2048], BF16, A_XL + XL_WD)
    A.ostg = at("ostg", [128, 1024], F32, A_XL + XL_OSTG)
    A.dsb = at("dsb", [128, D], BF16, A_XL + XL_DSB)
    A.dtmp = at("dtmp", [128, 512], BF16, A_XL + XL_TMP)
    A.QB_lns = at("QB_lns", [128, 4 * D], BF16, A_QB + QB_LNS)
    A.gtile = at("gtile", [128, 2 * D], mybir.dt.uint8, A_QB + QB_GT)
    A.utile = at("utile", [128, 2 * D], mybir.dt.uint8, A_QB + QB_UT)
    A.gtile_c = at("gtile_c", [128, 2 * D], BF16, A_QB + QB_GTC)
    A.utile_c = at("utile_c", [128, 2 * D], BF16, A_QB + QB_UTC)
    A.hstage = at("hstage", [128, 1024], BF16, A_QB + QB_HST)
    A.xstg = at("xstg", [128, 2 * D], F32, A_QB + QB_XSTG)
    A.gact = at("gact", [128, 1024], BF16, A_QB + QB_GACT)
    A.qrow = at("qrow", [128, 512], F32, A_ST + ST_QROW)
    A.iota512 = at("iota512", [128, 512], F32, A_ST + ST_IOTA)
    A.stats = at("stats", [128, 32], F32, A_ST + ST_STATS)
    A.rt1 = at("rt1", [64, 512], F32, A_ST + ST_RT)
    A.rt2 = at("rt2", [64, 512], F32, A_ST + ST_RT + 2 * KB)
    co = A_ST + ST_CONST
    A.idt = at("idt", [128, 128], BF16, co); co += 256
    A.kcol = at("kcol", [128, 32], F32, co); co += 128
    A.gs = at("gs", [128, 64], F32, co); co += 256
    A.us = at("us", [128, 64], F32, co); co += 256
    A.bqkv = at("bqkv", [128, 48], F32, co); co += 192
    A.baseq = at("baseq", [128, 4], F32, co); co += 32
    A.onc_b = at("onc_b", [128, 1], BF16, co); co += 32
    A.eps = at("eps", [128, 1], F32, co); co += 32
    A.m128 = at("m128", [128, 1], F32, co); co += 32
    A.onr_b = at("onr_b", [1, 128], BF16, co); co += 256
    A.onr_f = at("onr_f", [1, 128], F32, co); co += 512
    A.recip = at("recip", [1, 512], F32, A_ST + 11 * KB)
    A.recipb = at("recipb", [1, 512], BF16, A_ST + 4 * KB + 256)
    A.row_b = at("row_b", [1, D], BF16, A_ST + ST_ROW)

    A.SA = ctx.enter_context(nc.psum_tensor("SA", [128, 1024], F32))
    A.SB = ctx.enter_context(nc.psum_tensor("SB", [128, 1024], F32))
    A.PO = ctx.enter_context(nc.psum_tensor("PO", [128, 512], F32))
    A.PD = ctx.enter_context(nc.psum_tensor("PD", [128, 512], F32))
    A.PT = [ctx.enter_context(nc.psum_tensor(f"PT{i}", [128, 1024], BF16))
            for i in range(2)]
    A.PB = [A.SA[:, 0:512], A.SA[:, 512:1024], A.SB[:, 0:512],
            A.SB[:, 512:1024], A.PO[:], A.PD[:]]

    sem_names = ["pe", "act", "dve", "ld", "gld", "gst", "cc", "gpc",
                 "wl0", "wl1", "gl0", "gl1", "dl0", "dl1", "kvl", "hl",
                 "rl", "wol", "st0", "st1", "xl0", "xl1", "dl2", "dl3"]
    A.sems = {n: ctx.enter_context(nc.semaphore(n)) for n in sem_names}

    with ctx:
        marks = {}
        emit_program(EM(marks), A)
        with nc.Block() as block:
            def run_for(engname):
                def f(eng):
                    emit_program(EM(marks, engines={engname: eng}, sems=A.sems), A)
                return f
            block.tensor(run_for('t'))
            block.scalar(run_for('s'))
            block.vector(run_for('v'))
            block.gpsimd(run_for('g'))
            block.sync(run_for('sy'))
    return nc



def kernel(L0_attn_a_x, L0_attn_a_Wq, L0_attn_a_bq, L0_attn_a_Wk, L0_attn_a_bk,
           L0_attn_a_Wv, L0_attn_a_bv, L0_attn_a_Wo, L0_attn_a_bo,
           L0_attn_b_Wq, L0_attn_b_Wk, L0_attn_b_Wv, L0_attn_b_Wp,
           L0_ffn_gate_w, L0_ffn_gate_scale, L0_ffn_up_w, L0_ffn_up_scale,
           L0_ffn_down_w, L0_ffn_down_scale):
    f32 = np.float32
    x = np.asarray(L0_attn_a_x, dtype=f32)

    def stack_heads(W):
        return np.asarray(W, f32).transpose(1, 0, 2).reshape(D, D)
    a_wq = _lhsT_tile(stack_heads(L0_attn_a_Wq))
    a_wk = _lhsT_tile(stack_heads(L0_attn_a_Wk))
    a_wv = _lhsT_tile(stack_heads(L0_attn_a_Wv))
    a_wo = np.ascontiguousarray(np.asarray(L0_attn_a_Wo, f32))
    a_bo = np.asarray(L0_attn_a_bo, f32).reshape(1, D)
    b_wq = _lhsT_tile(np.ascontiguousarray(np.asarray(L0_attn_b_Wq, f32).T))
    b_wk = _lhsT_tile(np.ascontiguousarray(np.asarray(L0_attn_b_Wk, f32).T))
    b_wv = _lhsT_tile(np.ascontiguousarray(np.asarray(L0_attn_b_Wv, f32).T))
    b_wp = np.ascontiguousarray(np.asarray(L0_attn_b_Wp, f32).T)
    def u8view(a):
        a = np.ascontiguousarray(np.asarray(a, np.int32))
        return a.view(np.uint8)[:, 0::4]
    g_w = _lhsT_tile(np.ascontiguousarray(u8view(L0_ffn_gate_w).T))
    u_w = _lhsT_tile(np.ascontiguousarray(u8view(L0_ffn_up_w).T))
    d_w = np.ascontiguousarray(u8view(L0_ffn_down_w).T)
    dsb = np.ascontiguousarray(np.broadcast_to(
        np.asarray(L0_ffn_down_scale, f32).reshape(1, D), (128, D)))
    gs = np.ascontiguousarray(np.asarray(L0_ffn_gate_scale, f32).reshape(64, 128).T)
    us = np.ascontiguousarray(np.asarray(L0_ffn_up_scale, f32).reshape(64, 128).T)
    bqkv = np.ascontiguousarray(np.concatenate(
        [np.asarray(b, f32).T for b in
         (L0_attn_a_bq, L0_attn_a_bk, L0_attn_a_bv)], axis=1))
    idt = np.eye(128, dtype=ml_dtypes.bfloat16)
    CHUNKS = ([0, 3, 4, 7], [1, 2, 5, 6])
    kcol = np.empty((128, 32), dtype=f32)
    for kt in range(32):
        chunk = CHUNKS[kt // 16][(kt % 16) // 4]
        kcol[:, kt] = chunk * 512 + (kt % 4) * 128 + np.arange(128, dtype=f32)
    onc_b = np.ones((128, 1), dtype=ml_dtypes.bfloat16)
    onr_b = np.ones((1, 128), dtype=ml_dtypes.bfloat16)
    onr_f = np.ones((1, 128), dtype=f32)
    eps = np.full((128, 1), 1e-5, dtype=f32)
    m128 = np.full((128, 1), -128.0, dtype=f32)

    shared = dict(a_wq=a_wq, a_wk=a_wk, a_wv=a_wv, a_wo=a_wo, a_bo=a_bo,
                  b_wq=b_wq, b_wk=b_wk, b_wv=b_wv, b_wp=b_wp,
                  g_w=g_w, u_w=u_w, d_w=d_w, dsb_in=dsb, gs_in=gs, us_in=us,
                  bqkv_in=bqkv, idt_in=idt, kcol_in=kcol, onc_b_in=onc_b,
                  onr_b_in=onr_b, onr_f_in=onr_f, eps_in=eps, m128_in=m128)

    inv_freq = (10000.0 ** (-np.arange(0, 64, dtype=np.float64) / 64.0))
    in_maps = []
    for c in range(8):
        b, hf = c >> 1, c & 1
        chunks = CHUNKS[hf]
        pos = np.concatenate([np.arange(ch * 512, (ch + 1) * 512)
                              for ch in chunks]).astype(np.float64)
        ang = pos[None, :] * inv_freq[:, None]
        d = dict(shared)
        d["x"] = np.ascontiguousarray(
            x[b].reshape(8, 512, D)[chunks].reshape(TOW, D))
        d["rcos"] = np.cos(ang).astype(f32)
        d["rsin"] = np.sin(ang).astype(f32)
        d["baseq_in"] = np.ascontiguousarray(np.broadcast_to(
            np.array([ch * 512.0 for ch in chunks], dtype=f32)[None, :],
            (128, 4)))
        in_maps.append(d)

    nc = build_nc()
    res = run_bass_kernel_spmd(nc, in_maps, core_ids=list(range(8)))
    out = np.empty((4, TSEQ, D), dtype=f32)
    for c in range(8):
        b, hf = c >> 1, c & 1
        y = res.results[c]["y"]
        for i, ch in enumerate(CHUNKS[hf]):
            out[b, ch * 512:(ch + 1) * 512, :] = y[i * 512:(i + 1) * 512]
    return out
